# version 03
# baseline (speedup 1.0000x reference)
import numpy as np
from contextlib import ExitStack
import concourse.bass as bass
import concourse.mybir as mybir
from concourse.bass_utils import run_bass_kernel_spmd
from concourse.replica_groups import maybe_share_collective_output_space

ds = bass.ds
F32 = mybir.dt.float32
BF16 = mybir.dt.bfloat16
I32 = mybir.dt.int32
ALU = mybir.AluOpType
AF = mybir.ActivationFunctionType
AX = mybir.AxisListType.X

B, S, D = 4, 2048, 2048
SH = S // 2
L0_HD, LORA = 64, 512
L0_INTER = 5504
L1_HD = 128
L1_INTER = 8192
EXPC = 16.0
L0_SCALE = 0.125
L1_RSCALE = float(np.float32(1.0) / np.float32(np.sqrt(np.float32(128.0))))

DEBUG = False
GW_SL = L0_INTER // 8
DW_SL = D // 8
FC_SL = L1_INTER // 8
PJ_SL = D // 8


class Prog:
    def __init__(self, nc, es):
        self.nc = nc
        self.ops = {e: [] for e in ["sync", "tensor", "vector", "scalar", "gpsimd"]}
        self.sem = {}
        self.cnt = {}
        for name in ["qs", "qg", "qa2", "sv", "sa", "st", "cc", "vd", "qz", "qsl", "qx"]:
            self.sem[name] = es.enter_context(nc.semaphore(name))
            self.cnt[name] = 0

    def emit(self, engine, fn):
        self.ops[engine].append(fn)

    def dma(self, q, out, in_, transpose=False):
        eng = {"qs": "sync", "qa2": "scalar", "qg": "gpsimd"}[q]
        self.cnt[q] += 16
        s = self.sem[q]
        if transpose:
            self.emit(eng, lambda e, out=out, in_=in_, s=s: e.dma_start(out=out, in_=in_, transpose=True).then_inc(s, 16))
        else:
            self.emit(eng, lambda e, out=out, in_=in_, s=s: e.dma_start(out=out, in_=in_).then_inc(s, 16))
        return self.cnt[q]

    def dyn_dma(self, q, out, in_fn):
        eng = {"qs": "sync", "qa2": "scalar", "qg": "gpsimd"}[q]
        self.cnt[q] += 16
        s = self.sem[q]
        self.emit(eng, lambda e, out=out, in_fn=in_fn, s=s: e.dma_start(out=out, in_=in_fn()).then_inc(s, 16))
        return self.cnt[q]

    def wait(self, engine, q, v):
        if q in ("qs", "qg", "qa2"):
            v = max(v, self.cnt[q])
        s = self.sem[q]
        self.emit(engine, lambda e, s=s, v=v: e.wait_ge(s, v))

    def op(self, engine, fn, q=None):
        if q is None:
            self.emit(engine, fn)
            return None
        self.cnt[q] += 1
        s = self.sem[q]
        self.emit(engine, lambda e, fn=fn, s=s: fn(e).then_inc(s, 1))
        return self.cnt[q]

    def vstep(self, fn):
        v = self.op("vector", fn, "vd")
        self.wait("vector", "vd", v)
        return v

    def marker(self, engine, q):
        self.cnt[q] += 1
        s = self.sem[q]
        self.emit(engine, lambda e, s=s: e.nop().then_inc(s, 1))
        return self.cnt[q]

    def quiesce(self):
        for q in ("sv", "sa", "st", "vd"):
            self.wait("sync", q, self.cnt[q])
        for q in ("qs", "qg", "qa2", "qz", "qsl", "qx"):
            self.wait("sync", q, self.cnt[q])

    def cc(self, groups, in_t, out_t):
        self.cnt["cc"] += 1
        s = self.sem["cc"]
        self.emit("gpsimd", lambda e, groups=groups, in_t=in_t, out_t=out_t, s=s:
                  e.collective_compute("AllGather", ALU.bypass, replica_groups=groups,
                                       ins=[in_t.ap()], outs=[out_t.ap()]).then_inc(s, 1))
        return self.cnt["cc"]


def build_program():
    nc = bass.Bass("TRN2", target_bir_lowering=False)

    d_x0 = nc.dram_tensor("x0", [S, D], F32, kind="ExternalInput").ap()
    d_x0own = nc.dram_tensor("x0own", [SH, D], F32, kind="ExternalInput").ap()
    d_kva = nc.dram_tensor("kva", [D, LORA], F32, kind="ExternalInput").ap()
    d_kvbk = nc.dram_tensor("kvbk", [LORA, 128], F32, kind="ExternalInput").ap()
    d_kvbv = nc.dram_tensor("kvbv", [LORA, 128], F32, kind="ExternalInput").ap()
    d_ow = nc.dram_tensor("ow", [D, D], F32, kind="ExternalInput").ap()
    d_lut = nc.dram_tensor("lut", [1, 16], F32, kind="ExternalInput").ap()
    d_gidx = nc.dram_tensor("gidx", [GW_SL, D], I32, kind="ExternalInput").ap()
    d_uidx = nc.dram_tensor("uidx", [GW_SL, D], I32, kind="ExternalInput").ap()
    d_didx = nc.dram_tensor("didx", [DW_SL, L0_INTER], I32, kind="ExternalInput").ap()
    d_wq = nc.dram_tensor("wq", [D, 8 * L1_HD], F32, kind="ExternalInput").ap()
    d_wk = nc.dram_tensor("wk", [D, 8 * L1_HD], F32, kind="ExternalInput").ap()
    d_wv = nc.dram_tensor("wv", [D, 8 * L1_HD], F32, kind="ExternalInput").ap()
    d_wo = nc.dram_tensor("wo", [D, D], F32, kind="ExternalInput").ap()
    d_bq = nc.dram_tensor("bq", [128, 8], F32, kind="ExternalInput").ap()
    d_bk = nc.dram_tensor("bk", [128, 8], F32, kind="ExternalInput").ap()
    d_bv = nc.dram_tensor("bv", [1, 8 * L1_HD], F32, kind="ExternalInput").ap()
    d_bo = nc.dram_tensor("bo", [1, D], F32, kind="ExternalInput").ap()
    d_wfc = nc.dram_tensor("wfc", [FC_SL, D], F32, kind="ExternalInput").ap()
    d_wpj = nc.dram_tensor("wpj", [PJ_SL, L1_INTER], F32, kind="ExternalInput").ap()
    d_cossin = nc.dram_tensor("cossin", [128, S], BF16, kind="ExternalInput").ap()
    d_rot = nc.dram_tensor("rot", [64, 64], BF16, kind="ExternalInput").ap()
    d_masks = nc.dram_tensor("masks", [128, 4 * 512], BF16, kind="ExternalInput").ap()
    d_ident = nc.dram_tensor("ident", [128, 128], F32, kind="ExternalInput").ap()
    d_kcol = nc.dram_tensor("kcol", [1, 16], F32, kind="ExternalInput").ap()
    d_qoff = nc.dram_tensor("qoff", [1, 1], I32, kind="ExternalInput").ap()

    d_out = nc.dram_tensor("out", [SH, D], F32, kind="ExternalOutput").ap()
    if DEBUG:
        dbg = {
            "o_xnd": nc.dram_tensor("o_xnd", [D, S], BF16, kind="ExternalOutput").ap(),
            "o_kv": nc.dram_tensor("o_kv", [128, 8192], BF16, kind="ExternalOutput").ap(),
            "o_kr": nc.dram_tensor("o_kr", [64, 4096], BF16, kind="ExternalOutput").ap(),
            "o_z0": nc.dram_tensor("o_z0", [SH, S], BF16, kind="ExternalOutput").ap(),
            "o_res": nc.dram_tensor("o_res", [SH, D], F32, kind="ExternalOutput").ap(),
            "o_x1": nc.dram_tensor("o_x1", [SH, D], F32, kind="ExternalOutput").ap(),
            "o_gw": nc.dram_tensor("o_gw", [GW_SL, D], BF16, kind="ExternalOutput").ap(),
            "o_fc": nc.dram_tensor("o_fc", [FC_SL, D], BF16, kind="ExternalOutput").ap(),
            "o_z1": nc.dram_tensor("o_z1", [SH, S], BF16, kind="ExternalOutput").ap(),
            "o_lut": nc.dram_tensor("o_lut", [128, 16], F32, kind="ExternalOutput").ap(),
            "o_z0o": nc.dram_tensor("o_z0o", [D, S], BF16, kind="ExternalOutput").ap(),
        }

    allsp = maybe_share_collective_output_space("AllGather", [list(range(8))], prefer_local=True)
    prsp = maybe_share_collective_output_space("AllGather", [[0, 1]])
    g_wg_in = nc.dram_tensor("g_wg_in", [GW_SL, D], BF16)
    g_wg_out = nc.dram_tensor("g_wg_out", [L0_INTER, D], BF16, addr_space=allsp)
    g_wu_in = nc.dram_tensor("g_wu_in", [GW_SL, D], BF16)
    g_wu_out = nc.dram_tensor("g_wu_out", [L0_INTER, D], BF16, addr_space=allsp)
    g_wd_in = nc.dram_tensor("g_wd_in", [DW_SL, L0_INTER], BF16)
    g_wd_out = nc.dram_tensor("g_wd_out", [D, L0_INTER], BF16, addr_space=allsp)
    g_fc_in = nc.dram_tensor("g_fc_in", [FC_SL, D], BF16)
    g_fc_out = nc.dram_tensor("g_fc_out", [L1_INTER, D], BF16, addr_space=allsp)
    g_pj_in = nc.dram_tensor("g_pj_in", [PJ_SL, L1_INTER], BF16)
    g_pj_out = nc.dram_tensor("g_pj_out", [D, L1_INTER], BF16, addr_space=allsp)
    pairs = [[2 * i, 2 * i + 1] for i in range(4)]
    g_z0_in = nc.dram_tensor("g_z0_in", [SH, S], BF16)
    g_z0_out = nc.dram_tensor("g_z0_out", [D, S], BF16, addr_space=prsp)
    g_xn_in = nc.dram_tensor("g_xn_in", [SH, D], F32)
    g_xn_out = nc.dram_tensor("g_xn_out", [S, D], F32, addr_space=prsp)
    g_z1_in = nc.dram_tensor("g_z1_in", [SH, S], BF16)
    g_z1_out = nc.dram_tensor("g_z1_out", [D, S], BF16, addr_space=prsp)
    d_kvab = nc.dram_tensor("kvab", [D, LORA], BF16)
    d_kvbkb = nc.dram_tensor("kvbkb", [LORA, 128], BF16)
    d_kvbvb = nc.dram_tensor("kvbvb", [LORA, 128], BF16)
    d_owb = nc.dram_tensor("owb", [D, D], BF16)
    d_wqb = nc.dram_tensor("wqb", [D, 8 * L1_HD], BF16)
    d_wkb = nc.dram_tensor("wkb", [D, 8 * L1_HD], BF16)
    d_wvb = nc.dram_tensor("wvb", [D, 8 * L1_HD], BF16)
    d_wob = nc.dram_tensor("wob", [D, D], BF16)
    d_x1s = nc.dram_tensor("x1s", [SH, D], F32)
    d_res = nc.dram_tensor("res", [SH, D], F32)
    d_xnd = nc.dram_tensor("xnd", [D, S], BF16)

    es = ExitStack()
    sb = lambda n, s, dt: es.enter_context(nc.sbuf_tensor(n, s, dt))

    xnT = sb("xnT", [128, 16 * S], BF16)
    bigA = sb("bigA", [128, 3072], F32)
    bigB = sb("bigB", [128, 16384], BF16)
    bigC = sb("bigC", [128, 16384], BF16)
    att = sb("att", [128, 16384], BF16)
    stage = sb("stage", [128, 2048], F32)
    slab = sb("slab", [128, 8192], BF16)
    masks = sb("masksS", [128, 4 * 512], BF16)
    ident = sb("identS", [128, 128], F32)
    expb = sb("expb", [128, 1024], BF16)
    onesf = sb("onesf", [1, 128], F32)
    onesrb = sb("onesrb", [1, 128], BF16)
    onesbC = sb("onesbC", [128, 1], BF16)
    biasq = sb("biasqS", [128, 8], F32)
    biask = sb("biaskS", [128, 8], F32)
    biasv = sb("biasvS", [1, 8 * L1_HD], BF16)
    kcol = sb("kcolS", [1, 16], F32)
    cc1 = sb("cc1", [1, 8], F32)
    scol = sb("scolS", [1, 16], F32)
    sbcS = sb("sbcS", [128, 16], F32)
    rsbc = sb("rsbcS", [128, 16], F32)
    r15bc = sb("r15bcS", [128, 1], F32)
    lutc = sb("lutcS", [128, 16], F32)
    nrm = sb("nrm", [128, 8], F32)
    rotm = sb("rotmS", [64, 64], BF16)
    cns = sb("cnsS", [128, 2], F32)
    qoffS = sb("qoffS", [1, 1], I32)

    bigA_i = bigA[:, :].bitcast(I32)
    qu_t = bigA[:, 0:512]
    qa_t = bigA[:, 512:1024]
    qd_t = bigA[:, 1024:1536]
    qf_t = bigA[:, 1536:2048]
    qint_t = bigA_i[:, 2048:2560]
    gbuf_t = bigA_i[:, 2560:3072]
    slab_f = slab[:, :].bitcast(F32)
    stats_t = slab_f[:, 0:2048]
    amax_t = slab_f[:, 2048:2176]
    base_t = slab_f[:, 2176:2304]
    rbase_t = slab_f[:, 2304:2432]
    best_t = slab_f[:, 2432:2560]
    sstar_t = slab_f[:, 2560:2688]
    rsst_t = slab_f[:, 2688:2816]
    sscl_t = slab_f[:, 2816:2944]
    updt_t = slab_f[:, 2944:3072]
    tmpq_t = slab_f[:, 3072:3200]
    gbf_t = bigB[:, 0:512]
    gtmp_t = bigB[:, 512:1024]
    gacc_t = bigB[:, 1024:1536]
    wbuf = att[:, :].bitcast(F32)
    rden_t = stage[0:1, 0:512]
    rbc_t = stage[0:128, 512:1024]

    pb = [es.enter_context(nc.psum_tensor(f"pb{i}", [128, 512], F32)) for i in range(6)]
    pbT = es.enter_context(nc.psum_tensor("pbT", [128, 128], F32))
    pbX = es.enter_context(nc.psum_tensor("pbX", [128, 64], F32))

    P = Prog(nc, es)
    allg = [list(range(8))]

    rqh = {}
    rq = lambda: rqh["v"]

    v_mk = P.dma("qs", masks[:, :], d_masks[:, :])
    v_id = P.dma("qs", ident[:, :], d_ident[:, :])
    v_kc = P.dma("qs", kcol[:, :], d_kcol[:, :])
    v_lut = P.dma("qs", lutc[0:1, 0:16], d_lut[:, :])
    v_bq = P.dma("qs", biasq[:, :], d_bq[:, :])
    v_bk = P.dma("qs", biask[:, :], d_bk[:, :])
    v_qo = P.dma("qs", qoffS[:, :], d_qoff[:, :])
    v_bv = P.dma("qg", biasv[:, :], d_bv[:, :])
    for (dst, srct) in [(d_kvab, d_kva), (d_kvbkb, d_kvbk), (d_kvbvb, d_kvbv),
                        (d_owb, d_ow), (d_wqb, d_wq), (d_wkb, d_wk), (d_wvb, d_wv), (d_wob, d_wo)]:
        P.dma("qg", dst[:, :], srct[:, :])
    v_precast = P.cnt["qg"]
    P.wait("scalar", "qg", v_precast)
    P.wait("sync", "qg", v_precast)
    P.wait("sync", "qs", v_qo)
    P.emit("sync", lambda e: rqh.__setitem__(
        "v", nc.values_load(qoffS[0:1, 0:1], engines=[mybir.EngineType.SP], min_val=0, max_val=1024, skip_runtime_bounds_check=True)))

    P.wait("vector", "qs", v_lut)
    P.vstep(lambda v: v.memset(cc1[:, 0:1], 5.0))
    P.vstep(lambda v: v.memset(cc1[:, 1:2], 15.0))
    P.vstep(lambda v: v.memset(cc1[:, 2:3], 1.0))
    P.vstep(lambda v: v.memset(onesf[:, :], 1.0))
    P.vstep(lambda v: v.memset(onesrb[:, :], 1.0))
    P.vstep(lambda v: v.memset(onesbC[:, :], 1.0))
    P.vstep(lambda v: v.memset(cns[:, 0:1], 1e-5))
    P.vstep(lambda v: v.memset(cns[:, 1:2], -EXPC))
    P.vstep(lambda v: v.reciprocal(out=cc1[:, 3:4], in_=cc1[:, 0:1]))
    P.vstep(lambda v: v.tensor_tensor(out=cc1[:, 4:5], in0=cc1[:, 3:4], in1=cc1[:, 3:4], op=ALU.add))
    P.vstep(lambda v: v.tensor_tensor(out=cc1[:, 5:6], in0=cc1[:, 2:3], in1=cc1[:, 4:5], op=ALU.subtract))
    P.vstep(lambda v: v.reciprocal(out=cc1[:, 6:7], in_=cc1[:, 1:2]))
    P.vstep(lambda v: v.tensor_tensor(out=cc1[:, 7:8], in0=cc1[:, 5:6], in1=cc1[:, 6:7], op=ALU.mult))
    P.vstep(lambda v: v.scalar_tensor_tensor(out=scol[:, :], in0=kcol[:, :], scalar=cc1[:, 7:8],
                                             in1=cc1[:, 4:5].broadcast_to([1, 16]), op0=ALU.mult, op1=ALU.add))
    P.vstep(lambda v: v.tensor_copy(out=scol[:, 15:16], in_=cc1[:, 2:3]))
    v_scol = P.marker("vector", "sv")

    P.wait("tensor", "sv", v_scol)
    P.wait("tensor", "qs", v_lut)
    P.op("tensor", lambda t: t.matmul(pbX[:, 0:16], onesf[0:1, :], scol[0:1, :], start=True, stop=True))
    P.op("tensor", lambda t: t.matmul(pbX[:, 16:32], onesf[0:1, :], lutc[0:1, 0:16], start=True, stop=True))
    v_bc = P.op("tensor", lambda t: t.matmul(pbX[:, 32:33], onesf[0:1, :], cc1[0:1, 6:7], start=True, stop=True), "st")
    P.wait("vector", "st", v_bc)
    P.vstep(lambda v: v.tensor_copy(out=sbcS[:, :], in_=pbX[:, 0:16]))
    P.vstep(lambda v: v.tensor_copy(out=lutc[:, :], in_=pbX[:, 16:32]))
    P.vstep(lambda v: v.tensor_copy(out=r15bc[:, :], in_=pbX[:, 32:33]))
    P.vstep(lambda v: v.reciprocal(out=rsbc[:, :], in_=sbcS[:, :]))
    v_consts = P.marker("vector", "sv")

    def gather_matrix(d_idx, g_in, rows, cols):
        nrc = (rows + 127) // 128
        cchunks = []
        c0 = 0
        while c0 < cols:
            w = min(512, cols - c0)
            cchunks.append((c0, w))
            c0 += w
        for rc in range(nrc):
            r0 = rc * 128
            rr = min(128, rows - r0)
            for (c0, w) in cchunks:
                if gather_matrix.vprev[0]:
                    P.wait("gpsimd", "sv", gather_matrix.vprev[0])
                vld = P.dma("qg", gbuf_t[0:rr, 0:w], d_idx[r0:r0 + rr, c0:c0 + w])
                P.wait("vector", "qg", vld)
                P.vstep(lambda v, rr=rr, w=w: v.tensor_copy(out=gbf_t[0:rr, 0:w], in_=gbuf_t[0:rr, 0:w]))
                gather_matrix.vprev[0] = P.marker("vector", "sv")
                P.vstep(lambda v, rr=rr, w=w: v.tensor_scalar(out=gacc_t[0:rr, 0:w], in0=gbf_t[0:rr, 0:w],
                                                              scalar1=0.0, scalar2=lutc[0:rr, 0:1],
                                                              op0=ALU.is_equal, op1=ALU.mult))
                for j in range(1, 16):
                    P.vstep(lambda v, j=j, rr=rr, w=w: v.tensor_scalar(
                        out=gtmp_t[0:rr, 0:w], in0=gbf_t[0:rr, 0:w], scalar1=float(j),
                        scalar2=lutc[0:rr, j:j + 1], op0=ALU.is_equal, op1=ALU.mult))
                    P.vstep(lambda v, rr=rr, w=w: v.tensor_tensor(
                        out=gacc_t[0:rr, 0:w], in0=gacc_t[0:rr, 0:w], in1=gtmp_t[0:rr, 0:w], op=ALU.add))
                vg = P.marker("vector", "sv")
                P.wait("sync", "sv", vg)
                vst = P.dma("qs", g_in[r0:r0 + rr, c0:c0 + w], gacc_t[0:rr, 0:w])
                P.wait("vector", "qs", vst)

    P.wait("vector", "sv", v_consts)
    gather_matrix.vprev = [0]
    gather_matrix(d_gidx, g_wg_in, GW_SL, D)
    gather_matrix(d_uidx, g_wu_in, GW_SL, D)
    gather_matrix(d_didx, g_wd_in, DW_SL, L0_INTER)
    P.wait("gpsimd", "qs", P.cnt["qs"])
    P.cc(allg, g_wg_in, g_wg_out)
    P.cc(allg, g_wu_in, g_wu_out)
    v_ag_ghost = P.cc(allg, g_wd_in, g_wd_out)

    def quant_matrix(d_w, g_in, rows, cols):
        half = rows // 2
        rt = half // 128
        nb = cols // 64
        nt = cols // 512
        for hf in range(2):
            r0 = hf * half
            last = None
            for i in range(rt):
                last = P.dma("qs", wbuf[:, i * cols:(i + 1) * cols], d_w[r0 + i * 128:r0 + (i + 1) * 128, :])
            P.wait("vector", "qs", last)
            for i in range(rt):
                P.vstep(lambda v, i=i: v.tensor_reduce(
                    out=amax_t[:, i * nb:(i + 1) * nb],
                    in_=wbuf[:, i * cols:(i + 1) * cols].rearrange("p (b w) -> p b w", w=64),
                    axis=AX, op=ALU.max, apply_absolute_value=True))
            P.vstep(lambda v: v.tensor_scalar(out=amax_t[:, :], in0=amax_t[:, :], scalar1=1e-8, scalar2=None, op0=ALU.max))
            P.vstep(lambda v: v.tensor_scalar(out=base_t[:, :], in0=amax_t[:, :], scalar1=r15bc[:, 0:1], scalar2=None, op0=ALU.mult))
            P.vstep(lambda v: v.reciprocal(out=rbase_t[:, :], in_=base_t[:, :]))
            for i in range(rt):
                for j in range(nt):
                    tile = wbuf[:, i * cols + j * 512: i * cols + (j + 1) * 512]
                    bsl = slice(i * nb + j * 8, i * nb + (j + 1) * 8)
                    P.vstep(lambda v, tile=tile, bsl=bsl: v.tensor_tensor(
                        out=qu_t[:, :].rearrange("p (b w) -> p b w", w=64),
                        in0=tile.rearrange("p (b w) -> p b w", w=64),
                        in1=rbase_t[:, bsl].rearrange("p (b o) -> p b o", o=1).broadcast_to([128, 8, 64]), op=ALU.mult))
                    P.vstep(lambda v: v.tensor_scalar(out=qa_t[:, :], in0=qu_t[:, :], scalar1=-1.0, scalar2=None, op0=ALU.mult))
                    P.vstep(lambda v: v.tensor_tensor(out=qa_t[:, :], in0=qa_t[:, :], in1=qu_t[:, :], op=ALU.max))
                    for k in range(16):
                        P.vstep(lambda v, k=k: v.tensor_scalar(out=qint_t[:, :], in0=qa_t[:, :],
                                                               scalar1=rsbc[:, k:k + 1], scalar2=15.0,
                                                               op0=ALU.mult, op1=ALU.min))
                        P.vstep(lambda v: v.tensor_copy(out=qf_t[:, :], in_=qint_t[:, :]))
                        P.vstep(lambda v, k=k: v.scalar_tensor_tensor(out=qd_t[:, :], in0=qf_t[:, :],
                                                                      scalar=sbcS[:, k:k + 1], in1=qa_t[:, :],
                                                                      op0=ALU.mult, op1=ALU.subtract))
                        P.vstep(lambda v: v.tensor_tensor(out=qd_t[:, :], in0=qd_t[:, :], in1=qd_t[:, :], op=ALU.mult))
                        P.vstep(lambda v, k=k, bsl=bsl: v.tensor_reduce(
                            out=stats_t[:, k * 128:(k + 1) * 128][:, bsl],
                            in_=qd_t[:, :].rearrange("p (b w) -> p b w", w=64), axis=AX, op=ALU.add))
            P.vstep(lambda v: v.tensor_reduce(out=tmpq_t[:, 0:1], in_=stats_t[:, 0:1], axis=AX, op=ALU.add))
            P.vstep(lambda v: v.tensor_copy(out=best_t[:, :], in_=stats_t[:, 0:128]))
            P.vstep(lambda v: v.tensor_copy(out=sstar_t[:, :], in_=sbcS[:, 0:1].broadcast_to([128, 128])))
            for k in range(1, 16):
                sl = stats_t[:, k * 128:(k + 1) * 128]
                P.vstep(lambda v, sl=sl: v.tensor_tensor(out=updt_t[:, :], in0=sl, in1=best_t[:, :], op=ALU.is_lt))
                P.vstep(lambda v, sl=sl: v.tensor_tensor(out=best_t[:, :], in0=best_t[:, :], in1=sl, op=ALU.min))
                P.vstep(lambda v, k=k: v.scalar_tensor_tensor(out=tmpq_t[:, :], in0=updt_t[:, :], scalar=sbcS[:, k:k + 1],
                                                              in1=sstar_t[:, :], op0=ALU.mult, op1=ALU.add))
                P.vstep(lambda v: v.tensor_tensor(out=updt_t[:, :], in0=updt_t[:, :], in1=sstar_t[:, :], op=ALU.mult))
                P.vstep(lambda v: v.tensor_tensor(out=sstar_t[:, :], in0=tmpq_t[:, :], in1=updt_t[:, :], op=ALU.subtract))
            P.vstep(lambda v: v.reciprocal(out=rsst_t[:, :], in_=sstar_t[:, :]))
            P.vstep(lambda v: v.tensor_tensor(out=sscl_t[:, :], in0=base_t[:, :], in1=sstar_t[:, :], op=ALU.mult))
            for i in range(rt):
                for j in range(nt):
                    tile = wbuf[:, i * cols + j * 512: i * cols + (j + 1) * 512]
                    otile = att[:, i * cols + j * 512: i * cols + (j + 1) * 512]
                    bsl = slice(i * nb + j * 8, i * nb + (j + 1) * 8)
                    P.vstep(lambda v, tile=tile, bsl=bsl: v.tensor_tensor(
                        out=qu_t[:, :].rearrange("p (b w) -> p b w", w=64),
                        in0=tile.rearrange("p (b w) -> p b w", w=64),
                        in1=rbase_t[:, bsl].rearrange("p (b o) -> p b o", o=1).broadcast_to([128, 8, 64]), op=ALU.mult))
                    P.vstep(lambda v, bsl=bsl: v.tensor_tensor(
                        out=qu_t[:, :].rearrange("p (b w) -> p b w", w=64),
                        in0=qu_t[:, :].rearrange("p (b w) -> p b w", w=64),
                        in1=rsst_t[:, bsl].rearrange("p (b o) -> p b o", o=1).broadcast_to([128, 8, 64]), op=ALU.mult))
                    P.vstep(lambda v: v.tensor_scalar(out=qu_t[:, :], in0=qu_t[:, :], scalar1=15.0, scalar2=-15.0,
                                                      op0=ALU.min, op1=ALU.max))
                    P.vstep(lambda v: v.tensor_copy(out=qint_t[:, :], in_=qu_t[:, :]))
                    P.vstep(lambda v: v.tensor_copy(out=qf_t[:, :], in_=qint_t[:, :]))
                    P.vstep(lambda v, otile=otile, bsl=bsl: v.tensor_tensor(
                        out=otile.rearrange("p (b w) -> p b w", w=64),
                        in0=qf_t[:, :].rearrange("p (b w) -> p b w", w=64),
                        in1=sscl_t[:, bsl].rearrange("p (b o) -> p b o", o=1).broadcast_to([128, 8, 64]), op=ALU.mult))
            vq = P.marker("vector", "sv")
            P.wait("sync", "sv", vq)
            last = None
            for i in range(rt):
                last = P.dma("qs", g_in[r0 + i * 128:r0 + (i + 1) * 128, :], att[:, i * cols:(i + 1) * cols])
            P.wait("vector", "qs", last)

    quant_matrix(d_wfc, g_fc_in, FC_SL, D)
    quant_matrix(d_wpj, g_pj_in, PJ_SL, L1_INTER)
    P.wait("gpsimd", "qs", P.cnt["qs"])
    P.cc(allg, g_fc_in, g_fc_out)
    v_ag_wq = P.cc(allg, g_pj_in, g_pj_out)

    def rows_to_xnT(src_dram, n_tiles, do_norm, store_rows=None, store_xnd=True, do_transpose=True):
        for i in range(n_tiles):
            vld = P.dma("qs", stage[:, :], src_dram[i * 128:(i + 1) * 128, :])
            P.wait("vector", "qs", vld)
            if do_norm:
                P.wait("scalar", "qs", vld)
                P.vstep(lambda v: v.tensor_reduce(out=nrm[:, 0:1], in_=stage[:, :], axis=AX, op=ALU.add))
                va0 = P.op("scalar", lambda a: a.activation(out=att[:, 0:2048], in_=stage[:, :], func=AF.Square,
                                                            accum_out=nrm[:, 1:2]), "sa")
                P.wait("vector", "sa", va0)
                P.vstep(lambda v: v.tensor_scalar(out=nrm[:, 0:2], in0=nrm[:, 0:2], scalar1=1.0 / 2048.0, scalar2=None, op0=ALU.mult))
                P.vstep(lambda v: v.tensor_tensor(out=nrm[:, 2:3], in0=nrm[:, 0:1], in1=nrm[:, 0:1], op=ALU.mult))
                v_var = P.vstep(lambda v: v.tensor_tensor(out=nrm[:, 2:3], in0=nrm[:, 1:2], in1=nrm[:, 2:3], op=ALU.subtract))
                P.vstep(lambda v: v.tensor_scalar(out=nrm[:, 6:7], in0=nrm[:, 2:3], scalar1=cns[:, 0:1], scalar2=None, op0=ALU.add))
                P.wait("scalar", "vd", P.cnt["vd"])
                va = P.op("scalar", lambda a: a.activation(out=nrm[:, 3:4], in_=nrm[:, 2:3], func=AF.Sqrt, bias=cns[:, 0:1], scale=1.0), "sa")
                P.wait("vector", "sa", va)
                P.vstep(lambda v: v.reciprocal(out=nrm[:, 7:8], in_=nrm[:, 3:4]))
                P.vstep(lambda v: v.tensor_tensor(out=nrm[:, 7:8], in0=nrm[:, 6:7], in1=nrm[:, 7:8], op=ALU.mult))
                P.vstep(lambda v: v.tensor_tensor(out=nrm[:, 3:4], in0=nrm[:, 3:4], in1=nrm[:, 7:8], op=ALU.add))
                P.vstep(lambda v: v.tensor_scalar(out=nrm[:, 3:4], in0=nrm[:, 3:4], scalar1=0.5, scalar2=None, op0=ALU.mult))
                P.vstep(lambda v: v.reciprocal(out=nrm[:, 4:5], in_=nrm[:, 3:4]))
                P.vstep(lambda v: v.tensor_tensor(out=nrm[:, 5:6], in0=nrm[:, 0:1], in1=nrm[:, 4:5], op=ALU.mult))
                P.vstep(lambda v: v.tensor_scalar(out=nrm[:, 5:6], in0=nrm[:, 5:6], scalar1=-1.0, scalar2=None, op0=ALU.mult))
                v_nr = P.vstep(lambda v: v.tensor_scalar(out=stage[:, :], in0=stage[:, :], scalar1=nrm[:, 4:5],
                                                         scalar2=nrm[:, 5:6], op0=ALU.mult, op1=ALU.add))
                P.wait("scalar", "vd", v_nr)
                va2 = P.marker("scalar", "sa")
                src_buf = stage
            else:
                P.wait("scalar", "qs", vld)
                va2 = P.marker("scalar", "sa")
                src_buf = stage
            if store_rows is not None:
                P.wait("sync", "sa", va2)
                vs = P.dma("qs", store_rows[i * 128:(i + 1) * 128, :], src_buf[:, :])
                P.wait("vector", "qs", vs)
            if do_transpose:
                P.wait("tensor", "sa", va2)
                for c in range(16):
                    vt = P.op("tensor", lambda t, c=c, src_buf=src_buf: t.transpose(
                        pbT[:, :], src_buf[:, c * 128:(c + 1) * 128], ident[:, :]), "st")
                    P.wait("scalar", "st", vt)
                    va3 = P.op("scalar", lambda a, c=c, i=i: a.activation(
                        out=xnT[:, c * S + i * 128: c * S + (i + 1) * 128], in_=pbT[:, :], func=AF.Copy), "sa")
                    P.wait("tensor", "sa", va3)
                vm = P.marker("scalar", "sa")
                P.wait("sync", "sa", vm)
        if store_xnd:
            last = None
            for c in range(16):
                last = P.dma("qs", d_xnd[c * 128:(c + 1) * 128, :], xnT[:, c * S:(c + 1) * S])
            P.wait("vector", "qs", last)
            P.wait("gpsimd", "qs", last)

    rows_to_xnT(d_x0, 16, do_norm=True)

    if DEBUG:
        P.quiesce()
        P.dma("qs", dbg["o_lut"][:, :], lutc[:, :])
        P.dma("qs", dbg["o_xnd"][:, :], d_xnd[:, :])
        P.dma("qs", dbg["o_gw"][:, :], g_wg_in[:, :])
        P.dma("qs", dbg["o_fc"][:, :], g_fc_in[:, :])
    P.wait("sync", "qg", v_precast)
    v_kva = P.dma("qs", slab[:, 0:8192].rearrange("p (c l) -> p c l", c=16),
                  d_kvab[:, :].rearrange("(c p) l -> p c l", p=128))
    P.wait("tensor", "qs", v_kva)
    P.wait("tensor", "sa", P.cnt["sa"])
    kvct = bigB[:, 0:8192]
    for m in range(4):
        for n in range(4):
            vmm = None
            for k in range(16):
                q = "st" if k == 15 else None
                vmm = P.op("tensor", lambda t, m=m, n=n, k=k: t.matmul(
                    pb[m % 2][:, :], slab[:, k * 512 + m * 128: k * 512 + (m + 1) * 128],
                    xnT[:, k * S + n * 512: k * S + (n + 1) * 512], start=(k == 0), stop=(k == 15)), q)
            P.wait("scalar", "st", vmm)
            va = P.op("scalar", lambda a, m=m, n=n: a.activation(
                out=kvct[:, m * 2048 + n * 512: m * 2048 + (n + 1) * 512], in_=pb[m % 2][:, :], func=AF.Copy), "sa")
            P.wait("tensor", "sa", va)

    v_kvdone = P.marker("tensor", "st")
    P.wait("sync", "st", v_kvdone)
    v_kvb = P.dma("qs", slab[:, 0:512].rearrange("p (c l) -> p c l", c=4),
                  d_kvbkb[:, :].rearrange("(c p) l -> p c l", p=128))
    v_kvb2 = P.dma("qs", slab[:, 512:1024].rearrange("p (c l) -> p c l", c=4),
                   d_kvbvb[:, :].rearrange("(c p) l -> p c l", p=128))
    P.wait("tensor", "qs", v_kvb2)
    P.wait("tensor", "sa", P.cnt["sa"])

    if DEBUG:
        vdbg = P.marker("scalar", "sa")
        P.wait("sync", "sa", vdbg)
        P.quiesce()
        P.dma("qs", dbg["o_kv"][:, :], kvct)
    kbf = [bigB[0:64, 8192 + g * 2048: 8192 + (g + 1) * 2048] for g in range(2)]
    for g in range(2):
        for n in range(4):
            vmm = None
            for k in range(4):
                q = "st" if k == 3 else None
                vmm = P.op("tensor", lambda t, g=g, n=n, k=k: t.matmul(
                    pb[2][0:64, :], slab[:, k * 128 + g * 64: k * 128 + g * 64 + 64],
                    kvct[:, k * 2048 + n * 512: k * 2048 + (n + 1) * 512], start=(k == 0), stop=(k == 3)), q)
            P.wait("scalar", "st", vmm)
            va = P.op("scalar", lambda a, g=g, n=n: a.activation(
                out=kbf[g][:, n * 512:(n + 1) * 512], in_=pb[2][0:64, :], func=AF.Copy), "sa")
            P.wait("tensor", "sa", va)

    P.wait("vector", "sa", P.cnt["sa"])
    P.wait("tensor", "sa", P.cnt["sa"])
    v_xnfree = P.marker("tensor", "st")
    P.wait("sync", "st", v_xnfree)
    cosdup = xnT[0:64, 0:2048]
    sindup = xnT[0:64, 2048:4096]
    v_cs = P.dma("qs", cosdup, d_cossin[0:64, :])
    v_cs = P.dma("qs", sindup, d_cossin[64:128, :])
    v_rt = P.dma("qs", rotm[:, :], d_rot[:, :])
    P.wait("vector", "qs", v_rt)
    P.wait("tensor", "qs", v_rt)
    ropew1 = bigA[0:64, 0:512]
    ropew2 = bigA[0:64, 512:1024]
    rotb = bigA[0:64, 1024:1280].bitcast(BF16)

    def rope64(src_bf16, dst_bf16):
        for cn in range(4):
            vr = P.op("tensor", lambda t, cn=cn: t.matmul(
                pb[5][0:64, :], rotm[:, :], src_bf16[:, cn * 512:(cn + 1) * 512], start=True, stop=True), "st")
            P.wait("scalar", "st", vr)
            var = P.op("scalar", lambda a: a.activation(out=rotb[:, :], in_=pb[5][0:64, :], func=AF.Copy), "sa")
            P.wait("vector", "sa", var)
            P.vstep(lambda v, cn=cn: v.tensor_tensor(out=ropew1, in0=src_bf16[:, cn * 512:(cn + 1) * 512],
                                                     in1=cosdup[:, cn * 512:(cn + 1) * 512], op=ALU.mult))
            vv = P.op("vector", lambda v, cn=cn: v.tensor_tensor(out=ropew2, in0=rotb[:, :],
                                                                 in1=sindup[:, cn * 512:(cn + 1) * 512], op=ALU.mult), "sv")
            P.wait("vector", "sv", vv)
            P.wait("tensor", "sv", vv)
            vo = P.op("vector", lambda v, cn=cn: v.tensor_tensor(
                out=dst_bf16[:, cn * 512:(cn + 1) * 512], in0=ropew1, in1=ropew2, op=ALU.add), "sv")
            P.wait("vector", "sv", vo)
            P.wait("scalar", "sv", vo)
        return P.marker("vector", "sv")

    krop = [bigB[0:64, 12288 + g * 2048: 12288 + (g + 1) * 2048] for g in range(2)]
    rope64(kbf[0], krop[0])
    v_krop = rope64(kbf[1], krop[1])

    vaug = bigC[:, 4096:4096 + 16 * 130]
    for sc in range(16):
        vmm = None
        for k in range(4):
            q = "st" if k == 3 else None
            vmm = P.op("tensor", lambda t, sc=sc, k=k: t.matmul(
                pb[3][:, 0:128], kvct[:, k * 2048 + sc * 128: k * 2048 + (sc + 1) * 128],
                slab[:, 512 + k * 128: 512 + (k + 1) * 128], start=(k == 0), stop=(k == 3)), q)
        P.wait("scalar", "st", vmm)
        P.op("scalar", lambda a, sc=sc: a.activation(out=vaug[:, sc * 130: sc * 130 + 64], in_=pb[3][:, 0:64], func=AF.Copy))
        va = P.op("scalar", lambda a, sc=sc: a.activation(out=vaug[:, sc * 130 + 65: sc * 130 + 129], in_=pb[3][:, 64:128], func=AF.Copy), "sa")
        P.wait("tensor", "sa", va)
    P.wait("vector", "sa", P.cnt["sa"])
    P.vstep(lambda v: v.memset(vaug[:, :].rearrange("p (c h) -> p c h", h=130)[:, :, 64:65], 1.0))
    P.vstep(lambda v: v.memset(vaug[:, :].rearrange("p (c h) -> p c h", h=130)[:, :, 129:130], 1.0))
    v_vaug = P.marker("vector", "sv")

    attT = att[:, :]
    vz_last = [0]
    qrop = bigC[0:64, 6176:8224]
    qslot = [bigC[0:64, 0:2048], bigC[0:64, 2048:4096]]
    v_qdone = [0, 0]

    for h in range(16):
        kv = h // 8
        qld = qslot[h % 2]
        if v_qdone[h % 2]:
            P.wait("sync", "sv", v_qdone[h % 2])
        P.cnt["qs"] += 16
        sqs = P.sem["qs"]
        P.emit("sync", lambda e, h=h, sqs=sqs, qld=qld: e.dma_start(
            out=qld, in_=d_xnd[64 * h:, :][ds(rq(), 64), :]).then_inc(sqs, 16))
        vq = P.cnt["qs"]
        P.wait("vector", "qs", vq)
        P.wait("tensor", "qs", vq)
        v_qr = rope64(qld, qrop)
        v_qdone[h % 2] = v_qr
        P.wait("tensor", "sv", max(v_qr, v_krop, v_vaug))
        for Q in range(4):
            nkt = 4 * (Q + 1)
            sc_v = [0] * nkt
            ex_v = [0] * nkt
            av_v = [0] * nkt
            for kt in range(nkt):
                pbs = pb[2 + (kt % 2)]
                if kt >= 2:
                    P.wait("tensor", "sa", ex_v[kt - 2])
                sc_v[kt] = P.op("tensor", lambda t, kt=kt, kv=kv, Q=Q, pbs=pbs: t.matmul(
                    pbs[:, :], krop[kv][:, kt * 128:(kt + 1) * 128],
                    qrop[:, Q * 512:(Q + 1) * 512], start=True, stop=True), "st")
                eb = expb[:, (kt % 2) * 512:(kt % 2) * 512 + 512]
                P.wait("scalar", "st", sc_v[kt])
                if kt >= 2:
                    P.wait("scalar", "st", av_v[kt - 2])
                ex_v[kt] = P.op("scalar", lambda a, pbs=pbs, eb=eb: a.activation(
                    out=eb, in_=pbs[:, :], func=AF.Exp, bias=cns[:, 1:2], scale=L0_SCALE), "sa")
                dj = kt - 4 * Q
                if dj >= 0:
                    P.wait("vector", "sa", ex_v[kt])
                    vmm = P.op("vector", lambda v, eb=eb, dj=dj: v.tensor_tensor(
                        out=eb, in0=eb, in1=masks[:, dj * 512:(dj + 1) * 512], op=ALU.mult), "sv")
                    P.wait("tensor", "sv", vmm)
                else:
                    P.wait("tensor", "sa", ex_v[kt])
                av_v[kt] = P.op("tensor", lambda t, kt=kt, kv=kv, eb=eb, nkt=nkt: t.matmul(
                    pb[4][0:65, :], vaug[:, kt * 130 + kv * 65: kt * 130 + kv * 65 + 65], eb,
                    start=(kt == 0), stop=(kt == nkt - 1)), "st")
            P.wait("vector", "st", av_v[nkt - 1])
            v_rc = P.vstep(lambda v: v.reciprocal(out=rden_t[:, :], in_=pb[4][64:65, :]))
            P.wait("tensor", "vd", v_rc)
            vb = P.op("tensor", lambda t: t.matmul(pb[5][0:64, :], onesf[0:1, 0:64], rden_t[:, :], start=True, stop=True), "st")
            P.wait("scalar", "st", vb)
            va4 = P.op("scalar", lambda a: a.activation(out=rbc_t[0:64, :], in_=pb[5][0:64, :], func=AF.Copy), "sa")
            P.wait("vector", "sa", va4)
            vz = P.op("vector", lambda v, h=h, Q=Q: v.tensor_tensor(
                out=attT[(h % 2) * 64:(h % 2) * 64 + 64, (h // 2) * S + Q * 512:(h // 2) * S + (Q + 1) * 512],
                in0=pb[4][0:64, :], in1=rbc_t[0:64, :], op=ALU.mult), "sv")
            P.wait("tensor", "sv", vz)
            vz_last[0] = vz

    if DEBUG:
        P.quiesce()
        P.dma("qs", dbg["o_kr"][0:64, 0:2048], krop[0])
        P.dma("qs", dbg["o_kr"][0:64, 2048:4096], krop[1])
    P.wait("sync", "sv", vz_last[0])
    last = None
    for c in range(8):
        last = P.dma("qs", g_z0_in[c * 128:(c + 1) * 128, :], attT[:, c * S:(c + 1) * S])
    P.wait("gpsimd", "qs", last)
    if DEBUG:
        P.quiesce()
        P.dma("qs", dbg["o_z0"][:, :], g_z0_in[:, :])
    v_ag_z0 = P.cc(pairs, g_z0_in, g_z0_out)

    zown = bigB[:, :]
    P.wait("sync", "cc", v_ag_z0)
    last = None
    for g4 in range(4):
        P.cnt["qz"] += 16
        sqz = P.sem["qz"]
        P.emit("sync", lambda e, g4=g4, sqz=sqz: e.dma_start(
            out=zown[:, g4 * 4096:(g4 + 1) * 4096].rearrange("p (c l) -> p c l", c=4),
            in_=g_z0_out[g4 * 512:(g4 + 1) * 512, :].rearrange("(c p) s -> p c s", p=128)[:, :, ds(rq(), 1024)]).then_inc(sqz, 16))
        last = P.cnt["qz"]
    v_zown = last

    def oproj(z_src, w_dram, res_src_is_x0, bias_row=None):
        for n in [3, 0, 1, 2]:
            P.cnt["qsl"] += 16
            sql = P.sem["qsl"]
            P.emit("scalar", lambda e, n=n, sql=sql, w_dram=w_dram: e.dma_start(
                out=slab[:, 0:8192].rearrange("p (c l) -> p c l", c=16),
                in_=w_dram[:, n * 512:(n + 1) * 512].rearrange("(c p) l -> p c l", p=128)).then_inc(sql, 16))
            vsl = P.cnt["qsl"]
            P.wait("tensor", "qsl", vsl)
            if n == 3:
                P.wait("tensor", "qz", v_zown)
            for m in range(8):
                vmm = None
                for k in range(16):
                    q = "st" if (k == 15 and bias_row is None) else None
                    vmm = P.op("tensor", lambda t, m=m, k=k: t.matmul(
                        pb[m % 2][:, :], z_src[:, k * 1024 + m * 128: k * 1024 + (m + 1) * 128],
                        slab[:, k * 512:(k + 1) * 512], start=(k == 0), stop=(k == 15 and bias_row is None)), q)
                if bias_row is not None:
                    vmm = P.op("tensor", lambda t, m=m, n=n: t.matmul(
                        pb[m % 2][:, :], onesrb[0:1, :], bias_row[0:1, n * 512:(n + 1) * 512],
                        start=False, stop=True), "st")
                src_r = d_x0own if res_src_is_x0 else d_x1s
                vx = P.dma("qs", stage[:, 1024:1536], src_r[m * 128:(m + 1) * 128, n * 512:(n + 1) * 512])
                rtile = stage[:, 1024:1536]
                P.wait("vector", "qs", vx)
                P.wait("vector", "st", vmm)
                vev = P.op("vector", lambda v, m=m, rtile=rtile: v.tensor_tensor(
                    out=stage[:, 0:512], in0=pb[m % 2][:, :], in1=rtile, op=ALU.add), "sv")
                P.wait("tensor", "sv", vev)
                P.wait("sync", "sv", vev)
                vst = P.dma("qs", d_res[m * 128:(m + 1) * 128, n * 512:(n + 1) * 512], stage[:, 0:512])
                P.wait("vector", "qs", vst)
            vsm = P.marker("tensor", "st")
            P.wait("scalar", "st", vsm)

    oproj(zown, d_owb, res_src_is_x0=True)

    if DEBUG:
        P.wait("sync", "qs", P.cnt["qs"])
        P.quiesce()
        P.dma("qs", dbg["o_res"][:, :], d_res[:, :])
        P.dma("qs", dbg["o_z0o"][:, :], g_z0_out[:, :])
    P.wait("sync", "cc", v_ag_ghost)
    P.wait("scalar", "cc", v_ag_ghost)
    xnTo = xnT[:, 0:16 * 1024]
    last = None
    for g4 in range(4):
        P.cnt["qx"] += 16
        sqx = P.sem["qx"]
        P.emit("sync", lambda e, g4=g4, sqx=sqx: e.dma_start(
            out=xnTo[:, g4 * 4096:(g4 + 1) * 4096].rearrange("p (c l) -> p c l", c=4),
            in_=d_xnd[g4 * 512:(g4 + 1) * 512, :].rearrange("(c p) s -> p c s", p=128)[:, :, ds(rq(), 1024)]).then_inc(sqx, 16))
        last = P.cnt["qx"]
    v_xnTo = last

    def ghost_ffn(sh):
        s0 = sh * 512
        hbuf = lambda ci: (bigB[:, ci * 512:(ci + 1) * 512] if ci < 32 else bigC[:, (ci - 32) * 512:(ci - 31) * 512])
        ic_list = [(i * 512, min(512, L0_INTER - i * 512)) for i in range((L0_INTER + 511) // 512)]
        for ici, (i0, iw) in enumerate(ic_list):
            nmi = iw // 128
            vg = None
            for k in range(16):
                vg = P.dma("qa2", slab[:, k * iw:(k + 1) * iw],
                           g_wg_out[i0:i0 + iw, k * 128:(k + 1) * 128], transpose=True)
            P.wait("tensor", "qa2", vg)
            if ici == 0:
                P.wait("tensor", "qx", v_xnTo)
            gm = None
            for mi in range(nmi):
                vmm = None
                for k in range(16):
                    q = "st" if k == 15 else None
                    vmm = P.op("tensor", lambda t, mi=mi, k=k, iw=iw: t.matmul(
                        pb[mi % 2][:, :], slab[:, k * iw + mi * 128: k * iw + (mi + 1) * 128],
                        xnTo[:, k * 1024 + s0: k * 1024 + s0 + 512], start=(k == 0), stop=(k == 15)), q)
                P.wait("scalar", "st", vmm)
                va = P.op("scalar", lambda a, mi=mi: a.activation(
                    out=att[:, mi * 512:(mi + 1) * 512], in_=pb[mi % 2][:, :], func=AF.Silu), "sa")
                P.wait("tensor", "sa", va)
                gm = va
            vgm = P.marker("tensor", "st")
            P.wait("scalar", "st", vgm)
            vu = None
            for k in range(16):
                vu = P.dma("qa2", slab[:, k * iw:(k + 1) * iw],
                           g_wu_out[i0:i0 + iw, k * 128:(k + 1) * 128], transpose=True)
            P.wait("tensor", "qa2", vu)
            for mi in range(nmi):
                vmm = None
                for k in range(16):
                    q = "st" if k == 15 else None
                    vmm = P.op("tensor", lambda t, mi=mi, k=k, iw=iw: t.matmul(
                        pb[2 + (mi % 2)][:, :], slab[:, k * iw + mi * 128: k * iw + (mi + 1) * 128],
                        xnTo[:, k * 1024 + s0: k * 1024 + s0 + 512], start=(k == 0), stop=(k == 15)), q)
                P.wait("vector", "st", vmm)
                P.wait("vector", "sa", gm)
                ci = (i0 // 128) + mi
                vh = P.op("vector", lambda v, mi=mi, ci=ci: v.tensor_tensor(
                    out=hbuf(ci), in0=pb[2 + (mi % 2)][:, :], in1=att[:, mi * 512:(mi + 1) * 512], op=ALU.mult), "sv")
                P.wait("tensor", "sv", vh)
            vum = P.marker("tensor", "st")
            P.wait("scalar", "st", vum)
        vhm = P.marker("vector", "sv")
        P.wait("tensor", "sv", vhm)
        kcks = list(range(43))
        groups = [kcks[0:16], kcks[16:32], kcks[32:43]]
        for n in range(4):
            for gi, grp in enumerate(groups):
                vsl = None
                for idx, kk in enumerate(grp):
                    vsl = P.dma("qa2", slab[:, idx * 512:(idx + 1) * 512],
                                g_wd_out[n * 512:(n + 1) * 512, kk * 128:(kk + 1) * 128], transpose=True)
                P.wait("tensor", "qa2", vsl)
                for mi in range(4):
                    vmm = None
                    for idx, kk in enumerate(grp):
                        st_f = (gi == 0 and idx == 0)
                        sp_f = (gi == 2 and idx == len(grp) - 1)
                        q = "st" if sp_f else None
                        vmm = P.op("tensor", lambda t, mi=mi, idx=idx, kk=kk, st_f=st_f, sp_f=sp_f: t.matmul(
                            pb[mi][:, :], hbuf(kk)[:, mi * 128:(mi + 1) * 128],
                            slab[:, idx * 512:(idx + 1) * 512], start=st_f, stop=sp_f), q)
                vgm = P.marker("tensor", "st")
                P.wait("scalar", "st", vgm)
            for mi in range(4):
                vx = P.dma("qs", stage[:, 1024:1536], d_res[sh * 512 + mi * 128: sh * 512 + (mi + 1) * 128,
                                                            n * 512:(n + 1) * 512])
                P.wait("vector", "qs", vx)
                P.wait("vector", "st", P.cnt["st"])
                vev = P.op("vector", lambda v, mi=mi: v.tensor_tensor(
                    out=stage[:, 0:512], in0=pb[mi][:, :], in1=stage[:, 1024:1536], op=ALU.add), "sv")
                P.wait("sync", "sv", vev)
                P.wait("tensor", "sv", vev)
                vst = P.dma("qs", d_x1s[sh * 512 + mi * 128: sh * 512 + (mi + 1) * 128, n * 512:(n + 1) * 512],
                            stage[:, 0:512])
                P.wait("vector", "qs", vst)

    ghost_ffn(0)
    ghost_ffn(1)

    if DEBUG:
        P.wait("sync", "qs", P.cnt["qs"])
        P.quiesce()
        P.dma("qs", dbg["o_x1"][:, :], d_x1s[:, :])
    rows_to_xnT(d_x1s, 8, do_norm=True, store_rows=g_xn_in, store_xnd=False, do_transpose=False)
    P.wait("gpsimd", "qs", P.cnt["qs"])
    v_ag_xn = P.cc(pairs, g_xn_in, g_xn_out)
    P.wait("sync", "cc", v_ag_xn)
    rows_to_xnT(g_xn_out, 16, do_norm=False, store_xnd=True)

    P.wait("tensor", "sa", P.cnt["sa"])
    for h in range(8):
        vw = P.dma("qs", slab[:, 0:2048].rearrange("p (c l) -> p c l", c=16),
                   d_wkb[:, h * 128:(h + 1) * 128].rearrange("(c p) l -> p c l", p=128))
        P.wait("tensor", "qs", vw)
        for n in range(4):
            vmm = None
            for k in range(16):
                q = "st" if k == 15 else None
                vmm = P.op("tensor", lambda t, k=k, n=n: t.matmul(
                    pb[n % 2][:, :], slab[:, k * 128:(k + 1) * 128],
                    xnT[:, k * S + n * 512:(k) * S + n * 512 + 512], start=(k == 0), stop=(k == 15)), q)
            P.wait("vector", "st", vmm)
            va = P.op("vector", lambda v, h=h, n=n: v.tensor_scalar(
                out=bigB[:, h * 2048 + n * 512: h * 2048 + (n + 1) * 512], in0=pb[n % 2][:, :],
                scalar1=biask[:, h:h + 1], scalar2=None, op0=ALU.add), "sv")
            P.wait("tensor", "sv", va)
        vkm = P.marker("tensor", "st")
        P.wait("sync", "st", vkm)
    for g4 in range(2):
        vw = P.dma("qs", slab[:, 0:8192].rearrange("p (c l) -> p c l", c=16),
                   d_wvb[:, g4 * 512:(g4 + 1) * 512].rearrange("(c p) l -> p c l", p=128))
        P.wait("tensor", "qs", vw)
        P.wait("tensor", "qg", v_bv)
        for sc in range(16):
            vmm = None
            for k in range(16):
                vmm = P.op("tensor", lambda t, k=k, sc=sc: t.matmul(
                    pb[sc % 2][:, :], xnT[:, k * S + sc * 128: k * S + (sc + 1) * 128],
                    slab[:, k * 512:(k + 1) * 512], start=(k == 0), stop=False))
            vmm = P.op("tensor", lambda t, g4=g4, sc=sc: t.matmul(
                pb[sc % 2][:, :], onesrb[0:1, :], biasv[0:1, g4 * 512:(g4 + 1) * 512],
                start=False, stop=True), "st")
            P.wait("scalar", "st", vmm)
            va = P.op("scalar", lambda a, sc=sc, g4=g4: a.activation(
                out=bigC[:, sc * 1024 + g4 * 512: sc * 1024 + (g4 + 1) * 512], in_=pb[sc % 2][:, :], func=AF.Copy), "sa")
            P.wait("tensor", "sa", va)
        vvm = P.marker("tensor", "st")
        P.wait("sync", "st", vvm)
    P.wait("scalar", "sa", P.cnt["sa"])

    for h in range(8):
        vw = P.dma("qs", slab[:, 0:2048].rearrange("p (c l) -> p c l", c=16),
                   d_wqb[:, h * 128:(h + 1) * 128].rearrange("(c p) l -> p c l", p=128))
        P.wait("tensor", "qs", vw)
        for n in range(4):
            vmm = None
            for k in range(16):
                q = "st" if k == 15 else None
                vmm = P.op("tensor", lambda t, k=k, n=n: t.matmul(
                    pb[4][:, :], slab[:, k * 128:(k + 1) * 128],
                    xnT[:, k * S + n * 512: k * S + n * 512 + 512], start=(k == 0), stop=(k == 15)), q)
            P.wait("vector", "st", vmm)
            va = P.op("vector", lambda v, h=h, n=n: v.tensor_scalar(
                out=slab[:, 2048 + n * 512: 2048 + (n + 1) * 512], in0=pb[4][:, :],
                scalar1=biasq[:, h:h + 1], scalar2=None, op0=ALU.add), "sv")
            P.wait("tensor", "sv", va)
        qT = slab[:, 2048:4096]
        for Q in range(4):
            nkt = 4 * (Q + 1)
            sc_v = [0] * nkt
            ex_v = [0] * nkt
            av_v = [0] * nkt
            for kt in range(nkt):
                pbs = pb[2 + (kt % 2)]
                if kt >= 2:
                    P.wait("tensor", "sa", ex_v[kt - 2])
                sc_v[kt] = P.op("tensor", lambda t, kt=kt, h=h, Q=Q, pbs=pbs, qT=qT: t.matmul(
                    pbs[:, :], bigB[:, h * 2048 + kt * 128: h * 2048 + (kt + 1) * 128],
                    qT[:, Q * 512:(Q + 1) * 512], start=True, stop=True), "st")
                eb = expb[:, (kt % 2) * 512:(kt % 2) * 512 + 512]
                P.wait("scalar", "st", sc_v[kt])
                if kt >= 2:
                    P.wait("scalar", "st", av_v[kt - 2])
                ex_v[kt] = P.op("scalar", lambda a, pbs=pbs, eb=eb: a.activation(
                    out=eb, in_=pbs[:, :], func=AF.Exp, bias=cns[:, 1:2], scale=L1_RSCALE), "sa")
                dj = kt - 4 * Q
                if dj >= 0:
                    P.wait("vector", "sa", ex_v[kt])
                    vmm2 = P.op("vector", lambda v, eb=eb, dj=dj: v.tensor_tensor(
                        out=eb, in0=eb, in1=masks[:, dj * 512:(dj + 1) * 512], op=ALU.mult), "sv")
                    P.wait("tensor", "sv", vmm2)
                else:
                    P.wait("tensor", "sa", ex_v[kt])
                P.op("tensor", lambda t, kt=kt, h=h, eb=eb, nkt=nkt: t.matmul(
                    pb[4][:, :], bigC[:, kt * 1024 + h * 128: kt * 1024 + (h + 1) * 128], eb,
                    start=(kt == 0), stop=(kt == nkt - 1)))
                av_v[kt] = P.op("tensor", lambda t, kt=kt, eb=eb, nkt=nkt: t.matmul(
                    pb[5][0:1, :], onesbC[:, :], eb,
                    start=(kt == 0), stop=(kt == nkt - 1)), "st")
            P.wait("vector", "st", av_v[nkt - 1])
            v_rc = P.vstep(lambda v: v.reciprocal(out=rden_t[:, :], in_=pb[5][0:1, :]))
            P.wait("tensor", "vd", v_rc)
            vb = P.op("tensor", lambda t: t.matmul(pb[3][0:128, :], onesf[0:1, 0:128], rden_t[:, :], start=True, stop=True), "st")
            P.wait("scalar", "st", vb)
            va4 = P.op("scalar", lambda a: a.activation(out=rbc_t[0:128, :], in_=pb[3][0:128, :], func=AF.Copy), "sa")
            P.wait("vector", "sa", va4)
            vz = P.op("vector", lambda v, h=h, Q=Q: v.tensor_tensor(
                out=attT[:, h * 2048 + Q * 512: h * 2048 + (Q + 1) * 512],
                in0=pb[4][:, :], in1=rbc_t[0:128, :], op=ALU.mult), "sv")
            P.wait("tensor", "sv", vz)
            vz_last[0] = vz
        vhm = P.marker("tensor", "st")
        P.wait("sync", "st", vhm)

    P.wait("sync", "sv", vz_last[0])
    last = None
    for c in range(8):
        last = P.dma("qs", g_z1_in[c * 128:(c + 1) * 128, :], attT[:, c * 2048:(c + 1) * 2048])
    P.wait("gpsimd", "qs", last)
    if DEBUG:
        P.quiesce()
        P.dma("qs", dbg["o_z1"][:, :], g_z1_in[:, :])
    v_ag_z1 = P.cc(pairs, g_z1_in, g_z1_out)
    P.wait("sync", "cc", v_ag_z1)
    last = None
    for g4 in range(4):
        P.cnt["qz"] += 16
        sqz = P.sem["qz"]
        P.emit("sync", lambda e, g4=g4, sqz=sqz: e.dma_start(
            out=zown[:, g4 * 4096:(g4 + 1) * 4096].rearrange("p (c l) -> p c l", c=4),
            in_=g_z1_out[g4 * 512:(g4 + 1) * 512, :].rearrange("(c p) s -> p c s", p=128)[:, :, ds(rq(), 1024)]).then_inc(sqz, 16))
        last = P.cnt["qz"]
    v_zown = last

    biaso_t = att[0:1, 0:2048]
    v_bo = P.dma("qg", biaso_t, d_bo[:, :])
    P.wait("tensor", "qg", v_bo)
    oproj(zown, d_wob, res_src_is_x0=False, bias_row=biaso_t)

    last = None
    for g4 in range(4):
        P.cnt["qx"] += 16
        sqx = P.sem["qx"]
        P.emit("sync", lambda e, g4=g4, sqx=sqx: e.dma_start(
            out=xnTo[:, g4 * 4096:(g4 + 1) * 4096].rearrange("p (c l) -> p c l", c=4),
            in_=d_xnd[g4 * 512:(g4 + 1) * 512, :].rearrange("(c p) s -> p c s", p=128)[:, :, ds(rq(), 1024)]).then_inc(sqx, 16))
        last = P.cnt["qx"]
    v_xnTo = last
    P.wait("sync", "cc", v_ag_wq)
    P.wait("scalar", "cc", v_ag_wq)

    def l1_ffn(sh):
        s0 = sh * 512
        hbuf = lambda ci: (bigB[:, ci * 512:(ci + 1) * 512] if ci < 32 else bigC[:, (ci - 32) * 512:(ci - 31) * 512])
        for ic in range(16):
            vg = None
            for k in range(16):
                vg = P.dma("qa2", slab[:, k * 512:(k + 1) * 512],
                           g_fc_out[ic * 512:(ic + 1) * 512, k * 128:(k + 1) * 128], transpose=True)
            P.wait("tensor", "qa2", vg)
            if ic == 0:
                P.wait("tensor", "qx", v_xnTo)
            for mi in range(4):
                vmm = None
                for k in range(16):
                    q = "st" if k == 15 else None
                    vmm = P.op("tensor", lambda t, mi=mi, k=k: t.matmul(
                        pb[mi % 2][:, :], slab[:, k * 512 + mi * 128: k * 512 + (mi + 1) * 128],
                        xnTo[:, k * 1024 + s0: k * 1024 + s0 + 512], start=(k == 0), stop=(k == 15)), q)
                P.wait("scalar", "st", vmm)
                va = P.op("scalar", lambda a, mi=mi: a.activation(
                    out=att[:, mi * 512:(mi + 1) * 512], in_=pb[mi % 2][:, :], func=AF.Relu), "sa")
                P.wait("vector", "sa", va)
                ci = ic * 4 + mi
                vh = P.op("vector", lambda v, mi=mi, ci=ci: v.tensor_tensor(
                    out=hbuf(ci), in0=att[:, mi * 512:(mi + 1) * 512], in1=att[:, mi * 512:(mi + 1) * 512], op=ALU.mult), "sv")
                P.wait("tensor", "sv", vh)
            vgm = P.marker("tensor", "st")
            P.wait("scalar", "st", vgm)
        vhm = P.marker("vector", "sv")
        P.wait("tensor", "sv", vhm)
        groups = [list(range(g * 16, (g + 1) * 16)) for g in range(4)]
        for n in range(4):
            for gi, grp in enumerate(groups):
                vsl = None
                for idx, kk in enumerate(grp):
                    vsl = P.dma("qa2", slab[:, idx * 512:(idx + 1) * 512],
                                g_pj_out[n * 512:(n + 1) * 512, kk * 128:(kk + 1) * 128], transpose=True)
                P.wait("tensor", "qa2", vsl)
                for mi in range(4):
                    vmm = None
                    for idx, kk in enumerate(grp):
                        st_f = (gi == 0 and idx == 0)
                        sp_f = (gi == 3 and idx == 15)
                        q = "st" if sp_f else None
                        vmm = P.op("tensor", lambda t, mi=mi, idx=idx, kk=kk, st_f=st_f, sp_f=sp_f: t.matmul(
                            pb[mi][:, :], hbuf(kk)[:, mi * 128:(mi + 1) * 128],
                            slab[:, idx * 512:(idx + 1) * 512], start=st_f, stop=sp_f), q)
                vgm = P.marker("tensor", "st")
                P.wait("scalar", "st", vgm)
            for mi in range(4):
                vx = P.dma("qs", stage[:, 1024:1536], d_res[sh * 512 + mi * 128: sh * 512 + (mi + 1) * 128,
                                                            n * 512:(n + 1) * 512])
                P.wait("vector", "qs", vx)
                P.wait("vector", "st", P.cnt["st"])
                vev = P.op("vector", lambda v, mi=mi: v.tensor_tensor(
                    out=stage[:, 0:512], in0=pb[mi][:, :], in1=stage[:, 1024:1536], op=ALU.add), "sv")
                P.wait("sync", "sv", vev)
                P.wait("tensor", "sv", vev)
                vst = P.dma("qs", d_out[sh * 512 + mi * 128: sh * 512 + (mi + 1) * 128, n * 512:(n + 1) * 512],
                            stage[:, 0:512])
                P.wait("vector", "qs", vst)

    l1_ffn(0)
    l1_ffn(1)

    P.wait("sync", "qs", P.cnt["qs"])
    P.wait("sync", "qg", P.cnt["qg"])
    P.wait("sync", "qa2", P.cnt["qa2"])

    with nc.Block() as block:
        @block.sync
        def _(e):
            for fn in P.ops["sync"]:
                fn(e)

        @block.tensor
        def _(e):
            for fn in P.ops["tensor"]:
                fn(e)

        @block.vector
        def _(e):
            for fn in P.ops["vector"]:
                fn(e)

        @block.scalar
        def _(e):
            for fn in P.ops["scalar"]:
                fn(e)

        @block.gpsimd
        def _(e):
            for fn in P.ops["gpsimd"]:
                fn(e)

    es.close()
    return nc


def _aux_tables():
    import ml_dtypes
    inv_freq = 1.0 / (10000.0 ** (np.arange(0, 64, 2, dtype=np.float64) / 64.0))
    pos = np.arange(S, dtype=np.float64)
    ang = pos[None, :] * inv_freq[:, None]
    c = np.cos(ang); s = np.sin(ang)
    cossin = np.concatenate([c, c, s, s], axis=0).astype(np.float32).astype(ml_dtypes.bfloat16)
    rot = np.zeros((64, 64), np.float32)
    for m in range(32):
        rot[m + 32, m] = -1.0
    for m in range(32, 64):
        rot[m - 32, m] = 1.0
    rot = rot.astype(ml_dtypes.bfloat16)
    m = np.zeros((128, 4 * 512), np.float32)
    p = np.arange(128)[:, None]
    f = np.arange(512)[None, :]
    for d in range(4):
        m[:, d * 512:(d + 1) * 512] = (f - p >= 128 * d).astype(np.float32)
    masks = m.astype(ml_dtypes.bfloat16)
    ident = np.eye(128, dtype=np.float32)
    kcol = np.arange(16, dtype=np.float32)[None, :]
    return cossin, rot, masks, ident, kcol


def kernel(L0_attn_hidden_states, L0_attn_kv_a_w, L0_attn_kv_b_w, L0_attn_o_w,
           L0_ffn_lut, L0_ffn_gate_idx, L0_ffn_up_idx, L0_ffn_down_idx,
           L1_attn_W_Q, L1_attn_W_K, L1_attn_W_V, L1_attn_W_O,
           L1_attn_b_Q, L1_attn_b_K, L1_attn_b_V, L1_attn_b_O,
           L1_ffn_w_fc, L1_ffn_w_proj):
    cossin, rot, masks, ident, kcol = _aux_tables()
    x = np.ascontiguousarray(L0_attn_hidden_states, dtype=np.float32)
    kva = np.ascontiguousarray(L0_attn_kv_a_w, dtype=np.float32)
    kvb = np.ascontiguousarray(L0_attn_kv_b_w, dtype=np.float32)
    ow = np.ascontiguousarray(L0_attn_o_w, dtype=np.float32)
    lut = np.ascontiguousarray(L0_ffn_lut, dtype=np.float32).reshape(1, 16)
    gidx = np.ascontiguousarray(L0_ffn_gate_idx, dtype=np.int32)
    uidx = np.ascontiguousarray(L0_ffn_up_idx, dtype=np.int32)
    didx = np.ascontiguousarray(L0_ffn_down_idx, dtype=np.int32)
    WQ = np.ascontiguousarray(L1_attn_W_Q, dtype=np.float32)
    WK = np.ascontiguousarray(L1_attn_W_K, dtype=np.float32)
    WV = np.ascontiguousarray(L1_attn_W_V, dtype=np.float32)
    WO = np.ascontiguousarray(L1_attn_W_O, dtype=np.float32)
    bQ = np.ascontiguousarray(L1_attn_b_Q, dtype=np.float32)
    bK = np.ascontiguousarray(L1_attn_b_K, dtype=np.float32)
    bV = np.ascontiguousarray(L1_attn_b_V, dtype=np.float32)
    bO = np.ascontiguousarray(L1_attn_b_O, dtype=np.float32)
    wfc = np.ascontiguousarray(L1_ffn_w_fc, dtype=np.float32)
    wpj = np.ascontiguousarray(L1_ffn_w_proj, dtype=np.float32)

    wo_full = np.ascontiguousarray(WO.reshape(16 * 128, 2048))

    in_maps = []
    for c in range(8):
        b, h2 = c // 2, c % 2
        kvh = [2 * h2, 2 * h2 + 1]
        heads = list(range(8 * h2, 8 * h2 + 8))
        kvbk = np.ascontiguousarray(np.concatenate([kvb[:, 128 * g:128 * g + 64] for g in kvh], axis=1))
        kvbv = np.ascontiguousarray(np.concatenate([kvb[:, 128 * g + 64:128 * g + 128] for g in kvh], axis=1))
        in_maps.append({
            "x0": x[b],
            "x0own": x[b, 1024 * h2:1024 * h2 + 1024, :],
            "kva": kva,
            "kvbk": kvbk,
            "kvbv": kvbv,
            "ow": ow,
            "lut": lut,
            "gidx": gidx[c * GW_SL:(c + 1) * GW_SL],
            "uidx": uidx[c * GW_SL:(c + 1) * GW_SL],
            "didx": didx[c * DW_SL:(c + 1) * DW_SL],
            "wq": np.ascontiguousarray(np.concatenate([WQ[h] for h in heads], axis=1)),
            "wk": np.ascontiguousarray(np.concatenate([WK[h] for h in heads], axis=1)),
            "wv": np.ascontiguousarray(np.concatenate([WV[h] for h in heads], axis=1)),
            "wo": wo_full,
            "bq": np.ascontiguousarray(bQ[heads].T),
            "bk": np.ascontiguousarray(bK[heads].T),
            "bv": np.ascontiguousarray(bV[heads].reshape(1, -1)),
            "bo": np.ascontiguousarray(bO.reshape(1, -1)),
            "wfc": wfc[c * FC_SL:(c + 1) * FC_SL],
            "wpj": wpj[c * PJ_SL:(c + 1) * PJ_SL],
            "cossin": cossin,
            "rot": rot,
            "masks": masks,
            "ident": ident,
            "kcol": kcol,
            "qoff": np.array([[1024 * h2]], np.int32),
        })

    nc = build_program()
    res = run_bass_kernel_spmd(nc, in_maps, core_ids=list(range(8)))
    out = np.empty((B, S, D), np.float32)
    for c in range(8):
        b, h2 = c // 2, c % 2
        out[b, 1024 * h2:1024 * h2 + 1024, :] = res.results[c]["out"]
    return out


if __name__ == "__main__":
    build_program()
    print("build ok")
